# version 01 (final)
# baseline (speedup 1.0000x reference)
import re
import numpy as np
import ml_dtypes
import concourse.bass as bass
import concourse.mybir as mybir
from concourse.bass_utils import run_bass_kernel_spmd
from contextlib import ExitStack

F32 = mybir.dt.float32
BF16 = mybir.dt.bfloat16
AF = mybir.ActivationFunctionType
AL = mybir.AluOpType
AX = mybir.AxisListType

P = 128
D = 1024
H = 16
DH = 64
T = 2048
NT = 16
NC = 8
L = 4096
DHID = 4096
ABLK = [0, 3, 4, 7]
BBLK = [1, 2, 5, 6]
PADEXT = [2, 2, 4, 4, 6, 6, 8, 8]
NSLOT = 8
KB_ELEMS = NC * P * T
VB_ELEMS = T * D
BOUNCE_N = KB_ELEMS + VB_ELEMS + 128
KOFF, VOFF, NOFF = 0, KB_ELEMS, KB_ELEMS + VB_ELEMS

ALL_PHASES = ("l0attn", "l0ffn", "l1attn", "l1ffn")


class Sem:
    def __init__(self, es, nc, name):
        self.h = es.enter_context(nc.semaphore(name))
        self.v = 0
        self.base = 0

    def __call__(self, rel):
        return self.base + rel

    def inc(self, inst, k=1):
        inst.then_inc(self.h, k)
        self.v += k
        return self.v

    def dinc(self, inst):
        return self.inc(inst, 16)


def ctx_region(sub, l):
    idx = 8 * sub + l
    if idx < 7:
        return 0, 65 * idx
    if idx < 14:
        return 1, 65 * (idx - 7)
    return 2, 65 * (idx - 14)


def build(phases=ALL_PHASES, taps=()):
    nc = bass.Bass("TRN2", target_bir_lowering=False)
    es = ExitStack()

    def din(name, shape):
        return nc.dram_tensor(name, shape, F32, kind="ExternalInput").ap()

    x_in = din("x", [T, D])
    wq_in = din("wq", [D, D]); wk_in = din("wk", [D, D]); wv_in = din("wv", [D, D])
    wo_in = din("wo", [D, D]); bo_in = din("bo", [D])
    g1r_in = din("g1r", [1, 32, 4, 32]); g2r_in = din("g2r", [4, 16, 1, 16])
    g1i_in = din("g1i", [1, 32, 4, 32]); g2i_in = din("g2i", [4, 16, 1, 16])
    br_in = din("br", [512]); bi_in = din("bi", [512])
    wqkv_in = din("wqkv", [3 * D, D]); wout_in = din("wout", [D, D])
    w1_in = din("w1", [32, D, P]); w2_in = din("w2", [32, D, P])
    h1_in = din("h1", [32, DHID, P]); h2_in = din("h2", [32, DHID, P])
    w3_in = din("w3", [2, DHID, 512])
    s1_in = din("s1", [DHID]); s2_in = din("s2", [DHID])
    hs1_in = din("hs1", [DHID]); hs2_in = din("hs2", [DHID])
    s3_in = din("s3", [D])
    maskl0_in = nc.dram_tensor("maskl0", [NSLOT, 2, 4, P, 256], BF16, kind="ExternalInput").ap()
    ident_in = nc.dram_tensor("identc", [P, P], BF16, kind="ExternalInput").ap()
    trimask_in = nc.dram_tensor("trimask", [P, P], BF16, kind="ExternalInput").ap()
    flags_in = din("flags", [P, 2])

    y_out = nc.dram_tensor("y", [T, D], F32, kind="ExternalOutput").ap()

    tapout = {}
    def mktap(name, shape, dt):
        if name in taps:
            tapout[name] = nc.dram_tensor("tap_" + name, shape, dt, kind="ExternalOutput").ap()
    mktap("xln", [P, NC * T], BF16)
    mktap("lnstat", [P, NT * 6], F32)
    mktap("ctm", [P, NT * 512], BF16)

    wq_bf = nc.dram_tensor("wq_bf", [D, D], BF16)
    wk_bf = nc.dram_tensor("wk_bf", [D, D], BF16)
    wv_bf = nc.dram_tensor("wv_bf", [D, D], BF16)
    wo_bf = nc.dram_tensor("wo_bf", [D, D], BF16)
    wqkv_bf = nc.dram_tensor("wqkv_bf", [3 * D, D], BF16)
    w1_bf = nc.dram_tensor("w1_bf", [32, D, P], BF16)
    w2_bf = nc.dram_tensor("w2_bf", [32, D, P], BF16)
    h1_bf = nc.dram_tensor("h1_bf", [32, DHID, P], BF16)
    h2_bf = nc.dram_tensor("h2_bf", [32, DHID, P], BF16)
    w3_bf = nc.dram_tensor("w3_bf", [2, DHID, 512], BF16)
    wout_bf = nc.dram_tensor("wout_bf", [D, D], BF16)
    bounce_in = nc.dram_tensor("bounce_in", [BOUNCE_N], BF16)
    bounce_out = nc.dram_tensor("bounce_out", [2 * BOUNCE_N], BF16)
    t3_dram = nc.dram_tensor("t3_dram", [2, 1024, 256], F32)

    sbuf = lambda n, s, d: es.enter_context(nc.sbuf_tensor(n, s, d))
    xres = sbuf("xres", [P, NT * D], F32)
    xln = sbuf("xln", [P, NC * T], BF16)
    ident = sbuf("ident", [P, P], BF16)
    trimask = sbuf("trimaskb", [P, P], BF16)
    consts = sbuf("consts", [P, 8], F32)
    onescol = sbuf("onescol", [P, 2], BF16)
    onesrow = sbuf("onesrow", [1, P], BF16)
    ln_stat = sbuf("ln_stat", [P, NT * 6], F32)
    flags = sbuf("flagsb", [P, 2], F32)

    _real_sems = {}
    _ring = {"n": 0}

    def new_phase():
        _ring["n"] = 0

    def sem(n):
        key = f"r{_ring['n']:02d}"
        _ring["n"] += 1
        if key not in _real_sems:
            _real_sems[key] = Sem(es, nc, key)
        s = _real_sems[key]
        s.base = s.v
        return s

    def sem_d(n):
        if n not in _real_sems:
            _real_sems[n] = Sem(es, nc, n)
        s = _real_sems[n]
        return s

    s_x = sem_d("s_x")
    s_wc = sem_d("s_wc")
    s_cst = sem_d("s_cst")
    s_v = sem_d("s_v"); s_a = sem_d("s_a"); s_t = sem_d("s_t"); s_e = sem_d("s_e")
    s_st = sem_d("s_st")

    CAST_AT = {}
    PANEL_AT = {}
    _castsems = {}
    def cast_sem(nm):
        if nm not in _castsems:
            _castsems[nm] = Sem(es, nc, "cs_" + nm)
        return _castsems[nm]
    def wait_cast(eng, nm):
        s, v = CAST_AT[nm]
        eng.wait_ge(s.h, v)

    with nc.Block("ph0") as blk:
        @blk.gpsimd
        def _(g):
            casts = [("wk", wk_bf, wk_in), ("wv", wv_bf, wv_in), ("wq", wq_bf, wq_in),
                     ("wo", wo_bf, wo_in), ("wqkv", wqkv_bf, wqkv_in), ("wout", wout_bf, wout_in)]
            for nm, dst, srca in casts:
                cs = cast_sem(nm)
                CAST_AT[nm] = (cs, cs.dinc(g.dma_start(out=dst.ap(), in_=srca)))
            cs1, cs2 = cast_sem("w1"), cast_sem("w2")
            v1l, v2l = [], []
            for m in range(32):
                v1l.append(cs1.dinc(g.dma_start(out=w1_bf.ap()[m], in_=w1_in[m])))
                v2l.append(cs2.dinc(g.dma_start(out=w2_bf.ap()[m], in_=w2_in[m])))
            PANEL_AT["w1"] = (cs1, v1l); PANEL_AT["w2"] = (cs2, v2l)
            csh1 = cast_sem("h1"); vhl = []
            for m in range(32):
                vhl.append(csh1.dinc(g.dma_start(out=h1_bf.ap()[m], in_=h1_in[m])))
            PANEL_AT["h1"] = (csh1, vhl)
            csw3 = cast_sem("w3"); vwl = []
            for hh in range(2):
                vwl.append(csw3.dinc(g.dma_start(out=w3_bf.ap()[hh], in_=w3_in[hh])))
            PANEL_AT["w3"] = (csw3, vwl)
            csh2 = cast_sem("h2"); vh2l = []
            for m in range(32):
                vh2l.append(csh2.dinc(g.dma_start(out=h2_bf.ap()[m], in_=h2_in[m])))
            PANEL_AT["h2"] = (csh2, vh2l)

        @blk.sync
        def _(sy):
            s_x.dinc(sy.dma_start(out=xres[:], in_=x_in.rearrange("(t p) d -> p t d", p=P)))
            s_cst.dinc(sy.dma_start(out=ident[:], in_=ident_in))
            s_cst.dinc(sy.dma_start(out=trimask[:], in_=trimask_in))
            s_cst.dinc(sy.dma_start(out=flags[:], in_=flags_in))

        @blk.vector
        def _(v):
            v.memset(consts[:, 0:1], 1e-5)
            v.memset(consts[:, 1:2], 1.0)
            v.memset(consts[:, 2:3], 0.0)
            v.memset(onescol[:, 0:1], 1.0)
            v.memset(onescol[:, 1:2], 0.0)
            s_v.inc(v.memset(onesrow[:], 1.0))

    def emit_ln(tag, fence=None):
        new_phase()
        with nc.Block("ln" + tag) as blk, ExitStack() as ps:
            sq_scr = ps.enter_context(nc.sbuf_tensor("sq" + tag, [P, 2 * D], BF16))
            xlt = ps.enter_context(nc.sbuf_tensor("xlt" + tag, [P, 2 * D], BF16))
            pst = [ps.enter_context(nc.psum_tensor(f"pst{tag}{i}", [P, P], BF16)) for i in range(4)]
            s_r = sem("lnr" + tag)
            s_q = sem("lnq" + tag)
            s_s = sem("lns" + tag)
            s_z = sem("lnz" + tag)
            s_ap = sem("lna" + tag)
            s_tr = sem("lnt" + tag)
            s_ev = sem("lne" + tag)

            @blk.vector
            def _(v):
                v.wait_ge(s_x.h, s_x(16))
                for t in range(NT):
                    s_r.inc(v.tensor_reduce(ln_stat[:, t:t + 1], xres[:, D * t:D * (t + 1)],
                                            axis=AX.X, op=AL.add))

            @blk.scalar
            def _(a):
                a.wait_ge(s_x.h, s_x(16))
                if fence is not None:
                    a.wait_ge(fence[0], fence[1])
                for t in range(NT):
                    a.activation(sq_scr[:, D * (t % 2):D * (t % 2 + 1)],
                                 xres[:, D * t:D * (t + 1)], AF.Square,
                                 accum_out=ln_stat[:, NT + t:NT + t + 1])
                    s_q.inc(a.drain())

            @blk.vector
            def _(v):
                v.wait_ge(s_q.h, s_q(NT))
                v.drain()
                SU, SQ, ME = ln_stat[:, 0:NT], ln_stat[:, NT:2 * NT], ln_stat[:, 2 * NT:3 * NT]
                M2, VA = ln_stat[:, 4 * NT:5 * NT], ln_stat[:, 5 * NT:6 * NT]
                v.tensor_scalar(ME, SU, 1.0 / D, None, op0=AL.mult)
                v.drain()
                v.tensor_tensor(M2, ME, ME, op=AL.mult)
                v.drain()
                s_s.inc(v.scalar_tensor_tensor(VA, SQ, 1.0 / D, M2, op0=AL.mult, op1=AL.subtract))

            @blk.scalar
            def _(a):
                a.wait_ge(s_s.h, s_s(1))
                RS, VA = ln_stat[:, 3 * NT:4 * NT], ln_stat[:, 5 * NT:6 * NT]
                LNV = ln_stat[:, 4 * NT:5 * NT]
                a.activation(LNV, VA, AF.Ln, bias=consts[:, 0:1])
                a.drain()
                s_z.inc(a.activation(RS, LNV, AF.Exp, scale=-0.5))

            @blk.vector
            def _(v):
                v.wait_ge(s_z.h, s_z(1))
                for t in range(NT):
                    if t >= 2:
                        v.wait_ge(s_tr.h, s_tr(NC * (t - 1)))
                    s_ap.inc(v.tensor_scalar(xlt[:, D * (t % 2):D * (t % 2 + 1)],
                                             xres[:, D * t:D * (t + 1)],
                                             ln_stat[:, 2 * NT + t:2 * NT + t + 1],
                                             ln_stat[:, 3 * NT + t:3 * NT + t + 1],
                                             op0=AL.subtract, op1=AL.mult))
                    if t > 0:
                        for c in range(NC):
                            v.wait_ge(s_tr.h, s_tr(NC * (t - 1) + c + 1))
                            s_ev.inc(v.tensor_copy(xln[:, T * c + P * (t - 1): T * c + P * t],
                                                   pst[(NC * (t - 1) + c) % 4][:]))
                for c in range(NC):
                    v.wait_ge(s_tr.h, s_tr(NC * (NT - 1) + c + 1))
                    s_ev.inc(v.tensor_copy(xln[:, T * c + P * (NT - 1): T * c + P * NT],
                                           pst[(NC * (NT - 1) + c) % 4][:]))

            @blk.tensor
            def _(te):
                te.wait_ge(s_cst.h, 48)
                for t in range(NT):
                    te.wait_ge(s_ap.h, s_ap(t + 1))
                    for c in range(NC):
                        k = NC * t + c
                        if k >= 4:
                            te.wait_ge(s_ev.h, s_ev(k - 3))
                        s_tr.inc(te.transpose(pst[k % 4][:],
                                              xlt[:, D * (t % 2) + P * c: D * (t % 2) + P * (c + 1)],
                                              ident[:]))

    l0 = {}

    def emit_l0_kv(les):
        new_phase()
        l0["kmax2"] = les.enter_context(nc.sbuf_tensor("l0kmax2", [1, 16], F32))
        l0["bias"] = les.enter_context(nc.sbuf_tensor("l0bias", [P, 8], F32))
        l0["bohl"] = les.enter_context(nc.sbuf_tensor("l0bohl", [1, 2 * D], BF16))
        l0["ctm"] = les.enter_context(nc.sbuf_tensor("l0ctm", [P, NT * 512], BF16))
        l0["cc"] = sem_d("l0cc")

        with nc.Block("l0kv") as blk, ExitStack() as ps:
            wkT = ps.enter_context(nc.sbuf_tensor("l0wkT", [P, NC * D], BF16))
            wvT = ps.enter_context(nc.sbuf_tensor("l0wvT", [P, NC * D], BF16))
            kstg = ps.enter_context(nc.sbuf_tensor("l0kstg", [P, 2 * 512], BF16))
            vstg = ps.enter_context(nc.sbuf_tensor("l0vstg", [P, 2 * 512], BF16))
            sqk = ps.enter_context(nc.sbuf_tensor("l0sqk", [P, 2 * 512], BF16))
            misc = ps.enter_context(nc.sbuf_tensor("l0misc", [P, 64], F32))
            nstg = ps.enter_context(nc.sbuf_tensor("l0nstg", [1, 128], BF16))
            borow = ps.enter_context(nc.sbuf_tensor("l0bo", [1, D], F32))
            pk = [ps.enter_context(nc.psum_tensor(f"l0pk{i}", [P, 512], F32)) for i in range(2)]
            pv = [ps.enter_context(nc.psum_tensor(f"l0pv{i}", [P, 512], F32)) for i in range(2)]
            pn = [ps.enter_context(nc.psum_tensor(f"l0pn{i}", [P, 512], F32)) for i in range(2)]
            s_wl = sem("l0wl")
            s_mmk = sem("l0mmk"); s_sqk = sem("l0sqk"); s_evk = sem("l0evk")
            s_nmk = sem("l0nmk"); s_rmk = sem("l0rmk")
            s_dmk = [sem("l0dmka"), sem("l0dmkb")]
            s_mmv = sem("l0mmv"); s_evv = sem("l0evv")
            s_dmv = [sem("l0dmva"), sem("l0dmvb")]
            s_dmn = sem("l0dmn")
            s_pp = sem("l0pp")
            KU = [(h, t) for h in range(NC) for t in range(4)]
            VU = [(t, o) for t in range(NT) for o in range(2)]

            @blk.sync
            def _(sy):
                wait_cast(sy, "wk"); wait_cast(sy, "wv")
                for c in range(NC):
                    s_wl.dinc(sy.dma_start_transpose(wkT[:, D * c:D * (c + 1)],
                                                     wk_bf.ap()[:, P * c:P * (c + 1)]))
                for c in range(NC):
                    s_wl.dinc(sy.dma_start_transpose(wvT[:, D * c:D * (c + 1)],
                                                     wv_bf.ap()[:, P * c:P * (c + 1)]))
                s_wl.dinc(sy.dma_start(out=borow[0:1, :], in_=bo_in.rearrange("(o d) -> o d", o=1)))

            @blk.tensor
            def _(te):
                te.wait_ge(s_wl.h, s_wl(16 * 17))
                for u, (hp, ts) in enumerate(KU):
                    if u >= 2:
                        te.wait_ge(s_sqk.h, s_sqk(u - 1))
                        te.wait_ge(s_evk.h, s_evk(u - 1))
                    for dc in range(NC):
                        i = te.matmul(pk[u % 2][:], wkT[:, D * dc + P * hp:D * dc + P * (hp + 1)],
                                      xln[:, T * dc + 512 * ts:T * dc + 512 * (ts + 1)],
                                      start=(dc == 0), stop=(dc == NC - 1))
                    s_mmk.inc(i)
                    if u >= 1:
                        w = u - 1
                        te.wait_ge(s_sqk.h, s_sqk(w + 1))
                        if w >= 2:
                            te.wait_ge(s_rmk.h, s_rmk(w - 1))
                        s_nmk.inc(te.matmul(pn[w % 2][0:1, :], onescol[:, 0:1],
                                            sqk[:, 512 * (w % 2):512 * (w % 2 + 1)],
                                            start=True, stop=True))
                w = 31
                te.wait_ge(s_sqk.h, s_sqk(w + 1))
                te.wait_ge(s_rmk.h, s_rmk(w - 1))
                s_nmk.inc(te.matmul(pn[w % 2][0:1, :], onescol[:, 0:1],
                                    sqk[:, 512 * (w % 2):512 * (w % 2 + 1)],
                                    start=True, stop=True))
                for u, (tt, os) in enumerate(VU):
                    if u >= 2:
                        te.wait_ge(s_evv.h, s_evv(u - 1))
                    for dc in range(NC):
                        i = te.matmul(pv[u % 2][:], xln[:, T * dc + P * tt:T * dc + P * (tt + 1)],
                                      wvT[:, D * dc + 512 * os:D * dc + 512 * (os + 1)],
                                      start=(dc == 0), stop=(dc == NC - 1))
                    s_mmv.inc(i)

            @blk.scalar
            def _(a):
                for u in range(32):
                    a.wait_ge(s_mmk.h, s_mmk(u + 1))
                    if u >= 2:
                        a.wait_ge(s_nmk.h, s_nmk(u - 1))
                    a.activation(sqk[:, 512 * (u % 2):512 * (u % 2 + 1)], pk[u % 2][:], AF.Square)
                    s_sqk.inc(a.drain())

            @blk.vector
            def _(v):
                for u in range(32):
                    v.wait_ge(s_mmk.h, s_mmk(u + 1))
                    v.wait_ge(s_sqk.h, s_sqk(u + 1))
                    if u >= 2:
                        v.wait_ge(s_dmk[u % 2].h, s_dmk[u % 2](16 * (u // 2)))
                    s_evk.inc(v.tensor_copy(kstg[:, 512 * (u % 2):512 * (u % 2 + 1)], pk[u % 2][:]))
                    v.wait_ge(s_nmk.h, s_nmk(u + 1))
                    s_rmk.inc(v.tensor_reduce(misc[0:1, u:u + 1], pn[u % 2][0:1, :],
                                              axis=AX.X, op=AL.max))
                v.drain()
                for hp in range(NC):
                    v.tensor_reduce(misc[0:1, 32 + hp:33 + hp], misc[0:1, 4 * hp:4 * hp + 4],
                                    axis=AX.X, op=AL.max)
                v.drain()
                v.memset(nstg[0:1, :], 0.0)
                v.drain()
                v.tensor_copy(nstg[0:1, 0:8], misc[0:1, 32:40])
                v.wait_ge(s_wl.h, s_wl(16 * 17))
                v.tensor_copy(l0["bohl"][0:1, 0:D], borow[0:1, :])
                v.drain()
                s_pp.inc(v.tensor_tensor(l0["bohl"][0:1, D:2 * D], borow[0:1, :],
                                         l0["bohl"][0:1, 0:D], op=AL.subtract))
                for u in range(32):
                    v.wait_ge(s_mmv.h, s_mmv(u + 1))
                    if u >= 2:
                        v.wait_ge(s_dmv[u % 2].h, s_dmv[u % 2](16 * (u // 2)))
                    s_evv.inc(v.tensor_copy(vstg[:, 512 * (u % 2):512 * (u % 2 + 1)], pv[u % 2][:]))

            @blk.sync
            def _(sy):
                for u, (hp, ts) in enumerate(KU):
                    sy.wait_ge(s_evk.h, s_evk(u + 1))
                    s_dmk[u % 2].dinc(sy.dma_start(
                        out=bounce_in.ap()[KOFF + (P * T) * hp:KOFF + (P * T) * (hp + 1)]
                            .rearrange("(p t) -> p t", p=P)[:, 512 * ts:512 * (ts + 1)],
                        in_=kstg[:, 512 * (u % 2):512 * (u % 2 + 1)]))
                sy.wait_ge(s_pp.h, s_pp(1))
                s_dmn.dinc(sy.dma_start(out=bounce_in.ap()[NOFF:NOFF + 128].rearrange("(o n) -> o n", o=1),
                                        in_=nstg[0:1, :]))
                for u, (tt, os) in enumerate(VU):
                    sy.wait_ge(s_evv.h, s_evv(u + 1))
                    s_dmv[u % 2].dinc(sy.dma_start(
                        out=bounce_in.ap()[VOFF + (P * D) * tt:VOFF + (P * D) * (tt + 1)]
                            .rearrange("(p d) -> p d", p=P)[:, 512 * os:512 * (os + 1)],
                        in_=vstg[:, 512 * (u % 2):512 * (u % 2 + 1)]))

            marks = [(s.h, s.v) for s in (s_dmk[0], s_dmk[1], s_dmv[0], s_dmv[1], s_dmn)]

            @blk.gpsimd
            def _(g):
                for h, vv in marks:
                    g.wait_ge(h, vv)
                l0["cc"].inc(g.collective_compute(
                    "AllGather", AL.bypass,
                    replica_groups=[[0, 1], [2, 3], [4, 5], [6, 7]],
                    ins=[bounce_in.ap()], outs=[bounce_out.ap()]))

        with nc.Block("l0n") as blk, ExitStack() as ps:
            ntmp = ps.enter_context(nc.sbuf_tensor("l0ntmp", [1, 16], BF16))
            s_nl = sem("l0nl")

            @blk.sync
            def _(sy):
                sy.wait_ge(l0["cc"].h, l0["cc"].v)
                s_nl.dinc(sy.dma_start(out=ntmp[0:1, 0:8],
                                       in_=bounce_out.ap()[NOFF:NOFF + 8].rearrange("(o n) -> o n", o=1)))
                s_nl.dinc(sy.dma_start(out=ntmp[0:1, 8:16],
                                       in_=bounce_out.ap()[BOUNCE_N + NOFF:BOUNCE_N + NOFF + 8]
                                       .rearrange("(o n) -> o n", o=1)))

            @blk.vector
            def _(v):
                v.wait_ge(s_nl.h, s_nl(32))
                v.tensor_copy(l0["kmax2"][0:1, 0:8], ntmp[0:1, 0:8])
                v.tensor_copy(l0["kmax2"][0:1, 8:16], ntmp[0:1, 8:16])
                v.drain()
                v.tensor_tensor(l0["kmax2"][0:1, 0:8], l0["kmax2"][0:1, 0:8],
                                l0["kmax2"][0:1, 8:16], op=AL.max)

    def emit_l0_hp(hs, hpl):
        new_phase()
        hp = 4 * hs + hpl
        with nc.Block(f"l0h{hp}") as blk, ExitStack() as ps:
            wqT = ps.enter_context(nc.sbuf_tensor(f"l0wqT{hp}", [P, NC * P], BF16))
            qsb = ps.enter_context(nc.sbuf_tensor(f"l0qsb{hp}", [P, T], BF16))
            ksb = ps.enter_context(nc.sbuf_tensor(f"l0ksb{hp}", [P, L], BF16))
            vsb = ps.enter_context(nc.sbuf_tensor(f"l0vsb{hp}", [P, 32 * 130], BF16))
            sqq = ps.enter_context(nc.sbuf_tensor(f"l0sqq{hp}", [P, 2 * 512], BF16))
            misc = ps.enter_context(nc.sbuf_tensor(f"l0miscq{hp}", [P, 16], F32))
            nbf = ps.enter_context(nc.sbuf_tensor(f"l0nbf{hp}", [1, 4], BF16))
            biasb = ps.enter_context(nc.sbuf_tensor(f"l0biasb{hp}", [P, 4], F32))
            pta = ps.enter_context(nc.sbuf_tensor(f"l0pta{hp}", [P, 2 * 1024], BF16))
            ptb = ps.enter_context(nc.sbuf_tensor(f"l0ptb{hp}", [P, 2 * 1024], BF16))
            msb = ps.enter_context(nc.sbuf_tensor(f"l0msb{hp}", [P, 2 * 2048], BF16))
            den = ps.enter_context(nc.sbuf_tensor(f"l0den{hp}", [P, 8], F32))
            sca = ps.enter_context(nc.psum_tensor(f"l0sca{hp}", [P, 1024], F32))
            scb = ps.enter_context(nc.psum_tensor(f"l0scb{hp}", [P, 1024], F32))
            cps = [ps.enter_context(nc.psum_tensor(f"l0cp{hp}{i}", [P, 512], F32)) for i in range(2)]
            pq = [sca[:, 0:512], sca[:, 512:1024]]
            pn = [scb[:, 0:512], scb[:, 512:1024]]
            s_wl = sem(f"l0wl{hp}"); s_kl = sem(f"l0kl{hp}")
            s_mm = sem(f"l0mm{hp}"); s_sq = sem(f"l0sq{hp}"); s_ev = sem(f"l0ev{hp}")
            s_nm = sem(f"l0nm{hp}"); s_rm = sem(f"l0rm{hp}"); s_bb = sem(f"l0bb{hp}")
            s_sc = sem(f"l0sc{hp}"); s_ex = sem(f"l0ex{hp}"); s_ms = sem(f"l0ms{hp}")
            s_av = sem(f"l0av{hp}"); s_ce = sem(f"l0ce{hp}")
            s_ml = [sem(f"l0mla{hp}"), sem(f"l0mlb{hp}")]
            s_qd = sem(f"l0qd{hp}")
            ctm = l0["ctm"]

            units = []
            for g in range(NSLOT):
                for kg in range(PADEXT[g]):
                    units.append((g, kg))
            NU = len(units)
            ms_cnt = {}
            cum = 0
            for u, (g, kg) in enumerate(units):
                if kg >= PADEXT[g] - 2:
                    cum += 2
                ms_cnt[u] = cum
            g_first = {g: min(u for u, (gg, _) in enumerate(units) if gg == g) for g in range(NSLOT)}
            g_last = {g: max(u for u, (gg, _) in enumerate(units) if gg == g) for g in range(NSLOT)}

            @blk.sync
            def _(sy):
                wait_cast(sy, "wq")
                for c in range(NC):
                    s_wl.dinc(sy.dma_start_transpose(
                        wqT[:, P * c:P * (c + 1)],
                        wq_bf.ap()[P * hp:P * (hp + 1), P * c:P * (c + 1)]))
                for r in range(2):
                    blocks = ABLK if r == 0 else BBLK
                    for jb in range(4):
                        ab = blocks[jb]
                        s_kl.dinc(sy.dma_start(
                            out=ksb[:, 512 * ab:512 * (ab + 1)],
                            in_=bounce_out.ap()[r * BOUNCE_N + KOFF + (P * T) * hp:
                                                r * BOUNCE_N + KOFF + (P * T) * (hp + 1)]
                                .rearrange("(p t) -> p t", p=P)[:, 512 * jb:512 * (jb + 1)]))
                for r in range(2):
                    blocks = ABLK if r == 0 else BBLK
                    for tt in range(NT):
                        ab = blocks[tt // 4]
                        atile = 4 * ab + (tt % 4)
                        s_kl.dinc(sy.dma_start(
                            out=vsb[:, 130 * atile:130 * (atile + 1)]
                                .rearrange("p (l e) -> p l e", l=2)[:, :, 0:64],
                            in_=bounce_out.ap()[r * BOUNCE_N + VOFF + (P * D) * tt:
                                                r * BOUNCE_N + VOFF + (P * D) * (tt + 1)]
                                .rearrange("(p d) -> p d", p=P)[:, P * hp:P * (hp + 1)]))
                kl_all = s_kl.v
                for g in range(NSLOT):
                    if g >= 2:
                        sy.wait_ge(s_ms.h, s_ms(ms_cnt[g_last[g - 2]]))
                    for j in range(2):
                        s_ml[g % 2].dinc(sy.dma_start(
                            out=msb[:, 2048 * (g % 2) + 1024 * j:2048 * (g % 2) + 1024 * (j + 1)],
                            in_=maskl0_in[g, j].rearrange("kc p q -> p kc q")))

            @blk.tensor
            def _(te):
                te.wait_ge(s_wl.h, s_wl(16 * NC))
                for u in range(4):
                    if u >= 2:
                        te.wait_ge(s_sq.h, s_sq(u - 1))
                        te.wait_ge(s_ev.h, s_ev(u - 1))
                    for dc in range(NC):
                        i = te.matmul(pq[u % 2], wqT[:, P * dc:P * (dc + 1)],
                                      xln[:, T * dc + 512 * u:T * dc + 512 * (u + 1)],
                                      start=(dc == 0), stop=(dc == NC - 1))
                    s_mm.inc(i)
                    if u >= 1:
                        w = u - 1
                        te.wait_ge(s_sq.h, s_sq(w + 1))
                        if w >= 2:
                            te.wait_ge(s_rm.h, s_rm(w - 1))
                        s_nm.inc(te.matmul(pn[w % 2][0:1, :], onescol[:, 0:1],
                                           sqq[:, 512 * (w % 2):512 * (w % 2 + 1)],
                                           start=True, stop=True))
                w = 3
                te.wait_ge(s_sq.h, s_sq(w + 1))
                te.wait_ge(s_rm.h, s_rm(w - 1))
                s_nm.inc(te.matmul(pn[w % 2][0:1, :], onescol[:, 0:1],
                                   sqq[:, 512 * (w % 2):512 * (w % 2 + 1)],
                                   start=True, stop=True))
                te.wait_ge(s_bb.h, s_bb(1))
                s_nm.inc(te.matmul(pn[0][:, 0:1], onesrow[0:1, :], nbf[0:1, 0:1],
                                   start=True, stop=True))

            @blk.scalar
            def _(a):
                for u in range(4):
                    a.wait_ge(s_mm.h, s_mm(u + 1))
                    if u >= 2:
                        a.wait_ge(s_nm.h, s_nm(u - 1))
                    a.activation(sqq[:, 512 * (u % 2):512 * (u % 2 + 1)], pq[u % 2], AF.Square)
                    s_sq.inc(a.drain())
                a.wait_ge(s_rm.h, s_rm(6))
                s_sq.inc(a.activation(misc[0:1, 6:7], misc[0:1, 5:6], AF.Sqrt,
                                      scale=float((0.125 * 1.02) ** 2)))

            @blk.vector
            def _(v):
                for u in range(4):
                    v.wait_ge(s_mm.h, s_mm(u + 1))
                    v.wait_ge(s_sq.h, s_sq(u + 1))
                    s_ev.inc(v.tensor_copy(qsb[:, 512 * u:512 * (u + 1)], pq[u % 2]))
                    v.wait_ge(s_nm.h, s_nm(u + 1))
                    s_rm.inc(v.tensor_reduce(misc[0:1, u:u + 1], pn[u % 2][0:1, :],
                                             axis=AX.X, op=AL.max))
                v.drain()
                s_rm.inc(v.tensor_reduce(misc[0:1, 4:5], misc[0:1, 0:4], axis=AX.X, op=AL.max))
                v.drain()
                s_rm.inc(v.tensor_tensor(misc[0:1, 5:6], misc[0:1, 4:5],
                                         l0["kmax2"][0:1, hp:hp + 1], op=AL.mult))
                v.wait_ge(s_sq.h, s_sq(5))
                v.tensor_scalar(misc[0:1, 7:8], misc[0:1, 6:7], -1.0, None, op0=AL.mult)
                v.drain()
                s_bb.inc(v.tensor_copy(nbf[0:1, 0:1], misc[0:1, 7:8]))
                v.wait_ge(s_nm.h, s_nm(5))
                v.tensor_copy(biasb[:, 0:1], pn[0][:, 0:1])
                v.wait_ge(s_kl.h, s_kl(16 * 40))
                s_qd.inc(v.memset(vsb[:].rearrange("p (a l e) -> p a l e", a=32, l=2)[:, :, :, 64:65], 1.0))

            opened = {}

            def emit_av(te, u):
                g, kg = units[u]
                if u == g_first[g]:
                    opened.clear()
                if kg >= PADEXT[g] - 2:
                    te.wait_ge(s_ms.h, s_ms(ms_cnt[u]))
                else:
                    te.wait_ge(s_ex.h, s_ex(2 * (u + 1)))
                last = None
                for j in range(4):
                    atile = 4 * kg + j
                    for head, pt in ((0, pta), (1, ptb)):
                        for sub in range(2):
                            idx = 2 * sub + head
                            st = (len(opened) == 0)
                            opened[0] = True
                            closer = (u == g_last[g] and j == 3 and head == 1 and sub == 1)
                            last = te.matmul(
                                cps[g % 2][:, 65 * idx:65 * idx + 65],
                                pt[:, 1024 * (u % 2) + 256 * j + 128 * sub:
                                   1024 * (u % 2) + 256 * j + 128 * (sub + 1)],
                                vsb[:, 130 * atile + 65 * head:130 * atile + 65 * (head + 1)],
                                start=st, stop=closer,
                                skip_group_check=(not st) and (not closer))
                s_av.inc(last)

            @blk.tensor
            def _(te):
                te.wait_ge(s_qd.h, s_qd(1))
                for u, (g, kg) in enumerate(units):
                    if u >= 1:
                        emit_av(te, u - 1)
                    if u == g_first[g] and g >= 2:
                        te.wait_ge(s_ce.h, s_ce(g - 1))
                    te.wait_ge(s_ex.h, s_ex(2 * u))
                    for j in range(4):
                        kc = 512 * kg + 128 * j
                        te.matmul(sca[:, 256 * j:256 * (j + 1)],
                                  ksb[0:64, kc:kc + P],
                                  qsb[0:64, 256 * g:256 * (g + 1)],
                                  start=(j % 2 == 0), stop=True, tile_position=(0, 0),
                                  skip_group_check=(j % 2 == 1))
                        i = te.matmul(scb[:, 256 * j:256 * (j + 1)],
                                      ksb[64:128, kc:kc + P],
                                      qsb[64:128, 256 * g:256 * (g + 1)],
                                      start=(j % 2 == 0), stop=True, tile_position=(64, 0),
                                      skip_group_check=(j % 2 == 1))
                    s_sc.inc(i)
                emit_av(te, NU - 1)

            @blk.scalar
            def _(a):
                for u, (g, kg) in enumerate(units):
                    a.wait_ge(s_sc.h, s_sc(u + 1))
                    if u >= 2:
                        a.wait_ge(s_av.h, s_av(u - 1))
                    a.activation(pta[:, 1024 * (u % 2):1024 * (u % 2 + 1)], sca[:],
                                 AF.Exp, scale=0.125, bias=biasb[:, 0:1])
                    s_ex.inc(a.activation(ptb[:, 1024 * (u % 2):1024 * (u % 2 + 1)], scb[:],
                                          AF.Exp, scale=0.125, bias=biasb[:, 0:1]), 2)

            @blk.vector
            def _(v):
                for u, (g, kg) in enumerate(units):
                    if kg >= PADEXT[g] - 2:
                        j = kg - (PADEXT[g] - 2)
                        v.wait_ge(s_ml[g % 2].h, s_ml[g % 2](32 * (g // 2 + 1)))
                        v.wait_ge(s_ex.h, s_ex(2 * (u + 1)))
                        moff = 2048 * (g % 2) + 1024 * j
                        v.tensor_tensor(pta[:, 1024 * (u % 2):1024 * (u % 2 + 1)],
                                        pta[:, 1024 * (u % 2):1024 * (u % 2 + 1)],
                                        msb[:, moff:moff + 1024], op=AL.mult)
                        s_ms.inc(v.tensor_tensor(ptb[:, 1024 * (u % 2):1024 * (u % 2 + 1)],
                                                 ptb[:, 1024 * (u % 2):1024 * (u % 2 + 1)],
                                                 msb[:, moff:moff + 1024], op=AL.mult), 2)
                    if u == g_last[g]:
                        v.wait_ge(s_av.h, s_av(u + 1))
                        v.drain()
                        v.tensor_copy(den[:, 0:4],
                                      cps[g % 2][:, 0:260].rearrange("p (r c) -> p r c", c=65)[:, :, 64:65])
                        v.drain()
                        v.reciprocal(den[:, 4:8], den[:, 0:4])
                        v.drain()
                        for sub in range(2):
                            for head in range(2):
                                idx = 2 * sub + head
                                ll = 2 * hpl + head
                                i = v.tensor_scalar(
                                    ctm[:, 512 * (2 * g + sub) + 64 * ll:512 * (2 * g + sub) + 64 * (ll + 1)],
                                    cps[g % 2][:, 65 * idx:65 * idx + 64],
                                    den[:, 4 + idx:5 + idx], None, op0=AL.mult)
                        s_ce.inc(i)

    def emit_l0_oproj(hs):
        new_phase()
        with nc.Block(f"l0o{hs}") as blk, ExitStack() as ps:
            cfm = ps.enter_context(nc.sbuf_tensor(f"l0cfm{hs}", [P, 4 * T], BF16))
            woT = ps.enter_context(nc.sbuf_tensor(f"l0woT{hs}", [P, 4 * D], BF16))
            ptr = [ps.enter_context(nc.psum_tensor(f"l0ptr{hs}{i}", [P, P], BF16)) for i in range(4)]
            po = [ps.enter_context(nc.psum_tensor(f"l0po{hs}{i}", [P, 512], F32)) for i in range(2)]
            s_wl = sem(f"l0owl{hs}"); s_tr = sem(f"l0otr{hs}"); s_te = sem(f"l0ote{hs}")
            s_mm = sem(f"l0omm{hs}"); s_rs = sem(f"l0ors{hs}")
            ctm = l0["ctm"]

            @blk.sync
            def _(sy):
                wait_cast(sy, "wo")
                for c in range(4):
                    ec = 4 * hs + c
                    s_wl.dinc(sy.dma_start_transpose(woT[:, D * c:D * (c + 1)],
                                                     wo_bf.ap()[:, P * ec:P * (ec + 1)]))

            @blk.tensor
            def _(te):
                for t in range(NT):
                    for c in range(4):
                        k = 4 * t + c
                        if k >= 4:
                            te.wait_ge(s_te.h, s_te(k - 3))
                        s_tr.inc(te.transpose(ptr[k % 4][:],
                                              ctm[:, 512 * t + P * c:512 * t + P * (c + 1)],
                                              ident[:]))
                te.wait_ge(s_wl.h, s_wl(16 * 4))
                u = 0
                for tt in range(NT):
                    for oh in range(2):
                        if u >= 2:
                            te.wait_ge(s_rs.h, s_rs(u - 1))
                        for c in range(4):
                            te.wait_ge(s_te.h, s_te(4 * tt + c + 1))
                            i = te.matmul(po[u % 2][:], cfm[:, T * c + P * tt:T * c + P * (tt + 1)],
                                          woT[:, D * c + 512 * oh:D * c + 512 * (oh + 1)],
                                          start=(c == 0), stop=(hs == 1 and c == 3))
                        if hs == 0:
                            te.matmul(po[u % 2][:], onesrow[0:1, :],
                                      l0["bohl"][0:1, 512 * oh:512 * (oh + 1)],
                                      start=False, stop=False, skip_group_check=True)
                            i = te.matmul(po[u % 2][:], onesrow[0:1, :],
                                          l0["bohl"][0:1, D + 512 * oh:D + 512 * (oh + 1)],
                                          start=False, stop=True)
                        s_mm.inc(i)
                        u += 1

            @blk.vector
            def _(v):
                for k in range(NT * 4):
                    v.wait_ge(s_tr.h, s_tr(k + 1))
                    t, c = k // 4, k % 4
                    s_te.inc(v.tensor_copy(cfm[:, T * c + P * t:T * c + P * (t + 1)], ptr[k % 4][:]))
                u = 0
                for tt in range(NT):
                    for oh in range(2):
                        v.wait_ge(s_mm.h, s_mm(u + 1))
                        s_rs.inc(v.scalar_tensor_tensor(
                            xres[:, D * tt + 512 * oh:D * tt + 512 * (oh + 1)],
                            po[u % 2][:], 1.0,
                            xres[:, D * tt + 512 * oh:D * tt + 512 * (oh + 1)],
                            op0=AL.mult, op1=AL.add))
                        u += 1

        l0["rs_done"] = (s_rs.h, s_rs.v)

    def emit_l0_attn(sub):
        with ExitStack() as les:
            emit_l0_kv(les)
            for hs in range(2):
                for hpl in range(4):
                    if f"h{4*hs+hpl}" in sub:
                        emit_l0_hp(hs, hpl)
                if f"o{hs}" in sub:
                    emit_l0_oproj(hs)

    l0f = {}

    def emit_l0_wtt(les):
        new_phase()
        l0f["wtR"] = les.enter_context(nc.sbuf_tensor("wttR", [P, 4 * 512], BF16))
        l0f["wtI"] = les.enter_context(nc.sbuf_tensor("wttI", [P, 4 * 512], BF16))
        l0f["wtIn"] = les.enter_context(nc.sbuf_tensor("wttIn", [P, 4 * 512], BF16))
        with nc.Block("l0wtt") as blk, ExitStack() as ps:
            m1 = ps.enter_context(nc.sbuf_tensor("wtm1", [4, 2 * 1024], F32))
            m2 = ps.enter_context(nc.sbuf_tensor("wtm2", [4, 2 * 256], F32))
            m1r = ps.enter_context(nc.sbuf_tensor("wtm1r", [4, 2 * 1024], F32))
            tstg = ps.enter_context(nc.sbuf_tensor("wtts", [P, 2 * 256], F32))
            wstg = ps.enter_context(nc.sbuf_tensor("wtws", [P, 2 * 512], F32))
            pw = [ps.enter_context(nc.psum_tensor(f"wtp{i}", [P, 256], F32)) for i in range(2)]
            s_gl = sem("wtgl"); s_mr = sem("wtmr"); s_wm = sem("wtwm"); s_we = sem("wtwe")
            s_td = [sem("wttda"), sem("wttdb")]
            s_rb = [sem("wtrba"), sem("wtrbb")]
            s_cv = sem("wtcv")

            @blk.sync
            def _(sy):
                s_gl.dinc(sy.dma_start(out=m1[0:4, 0:1024], in_=g1r_in.rearrange("o b c d -> c b d")))
                s_gl.dinc(sy.dma_start(out=m1[0:4, 1024:2048], in_=g1i_in.rearrange("o b c d -> c b d")))
                s_gl.dinc(sy.dma_start(out=m2[0:4, 0:256], in_=g2r_in.rearrange("c e o g -> c e g")))
                s_gl.dinc(sy.dma_start(out=m2[0:4, 256:512], in_=g2i_in.rearrange("c e o g -> c e g")))

            @blk.vector
            def _(v):
                v.wait_ge(s_gl.h, s_gl(64))
                v.tensor_copy(m1r[0:4, 0:1024], m1[0:4, 0:1024].rearrange("c (b d) -> c d b", b=32))
                v.drain()
                s_mr.inc(v.tensor_copy(m1r[0:4, 1024:2048],
                                       m1[0:4, 1024:2048].rearrange("c (b d) -> c d b", b=32)))
                for k in range(16):
                    v.wait_ge(s_wm.h, s_wm(k + 1))
                    if k >= 2:
                        v.wait_ge(s_td[k % 2].h, s_td[k % 2](16 * (k // 2)))
                    s_we.inc(v.tensor_copy(tstg[:, 256 * (k % 2):256 * (k % 2 + 1)],
                                           pw[k % 2][:].rearrange("p (e g) -> p g e", e=16)))

            @blk.tensor
            def _(te):
                te.wait_ge(s_mr.h, s_mr(1))
                for k in range(16):
                    ri, ch = k // 8, k % 8
                    if k >= 2:
                        te.wait_ge(s_we.h, s_we(k - 1))
                    s_wm.inc(te.matmul(pw[k % 2][:], m1r[0:4, 1024 * ri + P * ch:1024 * ri + P * (ch + 1)],
                                       m2[0:4, 256 * ri:256 * (ri + 1)], start=True, stop=True))

            @blk.sync
            def _(sy):
                for k in range(16):
                    ri, ch = k // 8, k % 8
                    sy.wait_ge(s_we.h, s_we(k + 1))
                    s_td[k % 2].dinc(sy.dma_start(
                        out=t3_dram.ap()[ri, P * ch:P * (ch + 1), :],
                        in_=tstg[:, 256 * (k % 2):256 * (k % 2 + 1)]))
                sy.wait_ge(s_td[0].h, s_td[0].v)
                sy.wait_ge(s_td[1].h, s_td[1].v)
                for m in range(8):
                    ri, q = m // 4, m % 4
                    if m >= 2:
                        sy.wait_ge(s_cv.h, s_cv(m - 1))
                    for dd in range(8):
                        d = 8 * q + dd
                        s_rb[m % 2].dinc(sy.dma_start(
                            out=wstg[16 * dd:16 * (dd + 1), 512 * (m % 2):512 * (m % 2 + 1)],
                            in_=t3_dram.ap()[ri].rearrange("(d b) (g e) -> d g b e", d=32, b=32, g=16)[d]))

            @blk.vector
            def _(v):
                for m in range(8):
                    ri, q = m // 4, m % 4
                    v.wait_ge(s_rb[m % 2].h, s_rb[m % 2](16 * 8 * (m // 2 + 1)))
                    src = wstg[:, 512 * (m % 2):512 * (m % 2 + 1)]
                    if ri == 0:
                        s_cv.inc(v.tensor_copy(l0f["wtR"][:, 512 * q:512 * (q + 1)], src))
                    else:
                        v.tensor_copy(l0f["wtI"][:, 512 * q:512 * (q + 1)], src)
                        s_cv.inc(v.tensor_scalar(l0f["wtIn"][:, 512 * q:512 * (q + 1)], src,
                                                 -1.0, None, op0=AL.mult))

    def emit_l0_ffn():
        new_phase()
        with nc.Block("l0f") as blk, ExitStack() as ps:
            brbi = ps.enter_context(nc.sbuf_tensor("l0brbi", [1, 4 * 512], BF16))
            brf = ps.enter_context(nc.sbuf_tensor("l0brf", [1, 1024], F32))
            po = [ps.enter_context(nc.psum_tensor(f"l0fp{i}", [P, 512], F32)) for i in range(2)]
            s_bl = sem("l0fbl"); s_bp = sem("l0fbp"); s_mm = sem("l0fmm"); s_rs = sem("l0frs")
            wtR, wtI, wtIn = l0f["wtR"], l0f["wtI"], l0f["wtIn"]

            @blk.sync
            def _(sy):
                s_bl.dinc(sy.dma_start(out=brf[0:1, 0:512], in_=br_in.rearrange("(o d) -> o d", o=1)))
                s_bl.dinc(sy.dma_start(out=brf[0:1, 512:1024], in_=bi_in.rearrange("(o d) -> o d", o=1)))

            @blk.vector
            def _(v):
                v.wait_ge(s_bl.h, s_bl(32))
                v.tensor_copy(brbi[0:1, 0:512], brf[0:1, 0:512])
                v.tensor_copy(brbi[0:1, 1024:1536], brf[0:1, 512:1024])
                v.drain()
                v.tensor_tensor(brbi[0:1, 512:1024], brf[0:1, 0:512], brbi[0:1, 0:512], op=AL.subtract)
                s_bp.inc(v.tensor_tensor(brbi[0:1, 1536:2048], brf[0:1, 512:1024],
                                         brbi[0:1, 1024:1536], op=AL.subtract))
                u = 0
                for tt in range(NT):
                    for half in range(2):
                        v.wait_ge(s_mm.h, s_mm(u + 1))
                        s_rs.inc(v.scalar_tensor_tensor(
                            xres[:, D * tt + 512 * half:D * tt + 512 * (half + 1)],
                            po[u % 2][:], 1.0,
                            xres[:, D * tt + 512 * half:D * tt + 512 * (half + 1)],
                            op0=AL.mult, op1=AL.add))
                        u += 1

            @blk.tensor
            def _(te):
                te.wait_ge(s_bp.h, s_bp(1))
                u = 0
                for tt in range(NT):
                    for half in range(2):
                        if u >= 2:
                            te.wait_ge(s_rs.h, s_rs(u - 1))
                        w_for_xr = wtR if half == 0 else wtI
                        w_for_xi = wtIn if half == 0 else wtR
                        for q in range(4):
                            te.matmul(po[u % 2][:], xln[:, T * q + P * tt:T * q + P * (tt + 1)],
                                      w_for_xr[:, 512 * q:512 * (q + 1)],
                                      start=(q == 0), stop=False)
                        for q in range(4):
                            te.matmul(po[u % 2][:], xln[:, T * (4 + q) + P * tt:T * (4 + q) + P * (tt + 1)],
                                      w_for_xi[:, 512 * q:512 * (q + 1)],
                                      start=False, stop=False)
                        te.matmul(po[u % 2][:], onesrow[0:1, :],
                                  brbi[0:1, 1024 * half:1024 * half + 512],
                                  start=False, stop=False, skip_group_check=True)
                        i = te.matmul(po[u % 2][:], onesrow[0:1, :],
                                      brbi[0:1, 1024 * half + 512:1024 * half + 1024],
                                      start=False, stop=True)
                        s_mm.inc(i)
                        u += 1

        l0f["rs_done"] = (s_rs.h, s_rs.v)

    l1 = {}

    def emit_l1_kv(les):
        new_phase()
        l1["ctm"] = les.enter_context(nc.sbuf_tensor("l1ctm", [P, NT * D], BF16))
        l1["cc"] = sem_d("l1cc")
        with nc.Block("l1kv") as blk, ExitStack() as ps:
            wkT = ps.enter_context(nc.sbuf_tensor("l1wkT", [P, NC * D], BF16))
            wvT = ps.enter_context(nc.sbuf_tensor("l1wvT", [P, NC * D], BF16))
            kstg = ps.enter_context(nc.sbuf_tensor("l1kstg", [P, 2 * 512], BF16))
            vstg = ps.enter_context(nc.sbuf_tensor("l1vstg", [P, 2 * 512], BF16))
            mn = ps.enter_context(nc.sbuf_tensor("l1mn", [P, 2 * 512], F32))
            ex = ps.enter_context(nc.sbuf_tensor("l1ex", [P, 2 * 512], F32))
            pk = [ps.enter_context(nc.psum_tensor(f"l1pk{i}", [P, 512], F32)) for i in range(2)]
            pv = [ps.enter_context(nc.psum_tensor(f"l1pv{i}", [P, 512], F32)) for i in range(2)]
            s_wl = sem("l1wl")
            s_mmk = sem("l1mmk"); s_mnk = sem("l1mnk"); s_exk = sem("l1exk"); s_evk = sem("l1evk")
            s_dmk = [sem("l1dmka"), sem("l1dmkb")]
            s_mmv = sem("l1mmv"); s_evv = sem("l1evv")
            s_dmv = [sem("l1dmva"), sem("l1dmvb")]
            KU = [(t, o) for t in range(NT) for o in range(2)]

            @blk.sync
            def _(sy):
                wait_cast(sy, "wqkv")
                for c in range(NC):
                    s_wl.dinc(sy.dma_start_transpose(wkT[:, D * c:D * (c + 1)],
                                                     wqkv_bf.ap()[D:2 * D, P * c:P * (c + 1)]))
                for c in range(NC):
                    s_wl.dinc(sy.dma_start_transpose(wvT[:, D * c:D * (c + 1)],
                                                     wqkv_bf.ap()[2 * D:3 * D, P * c:P * (c + 1)]))

            @blk.tensor
            def _(te):
                te.wait_ge(s_wl.h, s_wl(16 * 16))
                for u, (tt, os) in enumerate(KU):
                    if u >= 2:
                        te.wait_ge(s_mnk.h, s_mnk(u - 1))
                        te.wait_ge(s_evk.h, s_evk(u - 1))
                    for dc in range(NC):
                        i = te.matmul(pk[u % 2][:], xln[:, T * dc + P * tt:T * dc + P * (tt + 1)],
                                      wkT[:, D * dc + 512 * os:D * dc + 512 * (os + 1)],
                                      start=(dc == 0), stop=(dc == NC - 1))
                    s_mmk.inc(i)
                for u, (tt, os) in enumerate(KU):
                    if u >= 2:
                        te.wait_ge(s_evv.h, s_evv(u - 1))
                    for dc in range(NC):
                        i = te.matmul(pv[u % 2][:], xln[:, T * dc + P * tt:T * dc + P * (tt + 1)],
                                      wvT[:, D * dc + 512 * os:D * dc + 512 * (os + 1)],
                                      start=(dc == 0), stop=(dc == NC - 1))
                    s_mmv.inc(i)

            @blk.scalar
            def _(a):
                for u in range(32):
                    a.wait_ge(s_mnk.h, s_mnk(u + 1))
                    if u >= 2:
                        a.wait_ge(s_exk.h, s_exk(u - 1))
                    s_exk.inc(a.activation(ex[:, 512 * (u % 2):512 * (u % 2 + 1)],
                                           mn[:, 512 * (u % 2):512 * (u % 2 + 1)], AF.Exp))

            @blk.vector
            def _(v):
                for u in range(32):
                    v.wait_ge(s_mmk.h, s_mmk(u + 1))
                    if u >= 2:
                        v.wait_ge(s_exk.h, s_exk(u - 1))
                    s_mnk.inc(v.tensor_scalar(mn[:, 512 * (u % 2):512 * (u % 2 + 1)],
                                              pk[u % 2][:], 0.0, None, op0=AL.min))
                    v.wait_ge(s_exk.h, s_exk(u + 1))
                    if u >= 2:
                        v.wait_ge(s_dmk[u % 2].h, s_dmk[u % 2](16 * (u // 2)))
                    s_evk.inc(v.scalar_tensor_tensor(kstg[:, 512 * (u % 2):512 * (u % 2 + 1)],
                                                     pk[u % 2][:], 0.0,
                                                     ex[:, 512 * (u % 2):512 * (u % 2 + 1)],
                                                     op0=AL.max, op1=AL.add))
                for u in range(32):
                    v.wait_ge(s_mmv.h, s_mmv(u + 1))
                    if u >= 2:
                        v.wait_ge(s_dmv[u % 2].h, s_dmv[u % 2](16 * (u // 2)))
                    s_evv.inc(v.tensor_copy(vstg[:, 512 * (u % 2):512 * (u % 2 + 1)], pv[u % 2][:]))

            @blk.sync
            def _(sy):
                for u, (tt, os) in enumerate(KU):
                    sy.wait_ge(s_evk.h, s_evk(u + 1))
                    s_dmk[u % 2].dinc(sy.dma_start(
                        out=bounce_in.ap()[KOFF + (P * D) * tt:KOFF + (P * D) * (tt + 1)]
                            .rearrange("(p d) -> p d", p=P)[:, 512 * os:512 * (os + 1)],
                        in_=kstg[:, 512 * (u % 2):512 * (u % 2 + 1)]))
                for u, (tt, os) in enumerate(KU):
                    sy.wait_ge(s_evv.h, s_evv(u + 1))
                    s_dmv[u % 2].dinc(sy.dma_start(
                        out=bounce_in.ap()[VOFF + (P * D) * tt:VOFF + (P * D) * (tt + 1)]
                            .rearrange("(p d) -> p d", p=P)[:, 512 * os:512 * (os + 1)],
                        in_=vstg[:, 512 * (u % 2):512 * (u % 2 + 1)]))

            marks = [(s.h, s.v) for s in (s_dmk[0], s_dmk[1], s_dmv[0], s_dmv[1])]

            @blk.gpsimd
            def _(g):
                for h, vv in marks:
                    g.wait_ge(h, vv)
                l1["cc"].inc(g.collective_compute(
                    "AllGather", AL.bypass,
                    replica_groups=[[0, 1], [2, 3], [4, 5], [6, 7]],
                    ins=[bounce_in.ap()[0:KB_ELEMS + VB_ELEMS]],
                    outs=[bounce_out.ap()[0:2 * (KB_ELEMS + VB_ELEMS)]]))

    def emit_l1_hp(hs, hpl):
        new_phase()
        hp = 4 * hs + hpl
        with nc.Block(f"l1h{hp}a") as blk, ExitStack() as ps:
            wqT = ps.enter_context(nc.sbuf_tensor(f"l1wqT{hp}", [P, NC * P], BF16))
            mn = ps.enter_context(nc.sbuf_tensor(f"l1mn{hp}", [P, 2 * 512], F32))
            ex = ps.enter_context(nc.sbuf_tensor(f"l1ex{hp}", [P, 2 * 512], F32))
            pq = [ps.enter_context(nc.psum_tensor(f"l1pq{hp}{i}", [P, 512], F32)) for i in range(2)]
            ptt = [ps.enter_context(nc.psum_tensor(f"l1ptt{hp}{i}", [P, P], BF16)) for i in range(2)]
            fq = l1["fq"]; fkt = l1["fkt"]; vau = l1["vau"]; fkf = l1["fkf"]
            vao = l1["vao"]; ssb = l1["ssb"]
            s_wl = sem(f"l1qwl{hp}"); s_kl = sem(f"l1kl{hp}")
            s_mm = sem(f"l1qmm{hp}"); s_mn = sem(f"l1qmn{hp}"); s_ex2 = sem(f"l1qex{hp}")
            s_ev = sem(f"l1qev{hp}"); s_bl = sem(f"l1bl{hp}")
            s_tr = sem(f"l1tr{hp}"); s_te = sem(f"l1te{hp}")

            @blk.sync
            def _(sy):
                wait_cast(sy, "wqkv")
                for c in range(NC):
                    s_wl.dinc(sy.dma_start_transpose(
                        wqT[:, P * c:P * (c + 1)],
                        wqkv_bf.ap()[P * hp:P * (hp + 1), P * c:P * (c + 1)]))
                sy.wait_ge(l1["cc"].h, l1["cc"].v)
                for r in range(2):
                    blocks = ABLK if r == 0 else BBLK
                    for tt in range(NT):
                        ab = blocks[tt // 4]
                        atile = 4 * ab + (tt % 4)
                        s_kl.dinc(sy.dma_start(
                            out=fkt[:, P * atile:P * (atile + 1)],
                            in_=bounce_out.ap()[r * (KB_ELEMS + VB_ELEMS) * 2 // 2 * 0 +
                                                r * 2 * (KB_ELEMS + VB_ELEMS) // 2 +
                                                KOFF + (P * D) * tt:
                                                r * (KB_ELEMS + VB_ELEMS) +
                                                KOFF + (P * D) * (tt + 1)]
                                .rearrange("(p d) -> p d", p=P)[:, P * hp:P * (hp + 1)]))
                        s_kl.dinc(sy.dma_start(
                            out=vau[:, 130 * atile:130 * (atile + 1)]
                                .rearrange("p (l e) -> p l e", l=2)[:, :, 0:64],
                            in_=bounce_out.ap()[r * (KB_ELEMS + VB_ELEMS) +
                                                VOFF + (P * D) * tt:
                                                r * (KB_ELEMS + VB_ELEMS) +
                                                VOFF + (P * D) * (tt + 1)]
                                .rearrange("(p d) -> p d", p=P)[:, P * hp:P * (hp + 1)]))
            kl_all = s_kl.v

            @blk.tensor
            def _(te):
                te.wait_ge(s_wl.h, s_wl(16 * NC))
                for u in range(4):
                    if u >= 2:
                        te.wait_ge(s_mn.h, s_mn(u - 1))
                        te.wait_ge(s_ev.h, s_ev(u - 1))
                    for dc in range(NC):
                        i = te.matmul(pq[u % 2][:], wqT[:, P * dc:P * (dc + 1)],
                                      xln[:, T * dc + 512 * u:T * dc + 512 * (u + 1)],
                                      start=(dc == 0), stop=(dc == NC - 1))
                    s_mm.inc(i)
                for j in range(NT):
                    te.wait_ge(s_bl.h, s_bl(j + 1))
                    if j >= 2:
                        te.wait_ge(s_te.h, s_te(j - 1))
                    s_tr.inc(te.transpose(ptt[j % 2][:], fkf[:, P * j:P * (j + 1)], ident[:]))

            @blk.scalar
            def _(a):
                for u in range(4):
                    a.wait_ge(s_mn.h, s_mn(u + 1))
                    if u >= 2:
                        a.wait_ge(s_ev.h, s_ev(u - 1))
                    s_ex2.inc(a.activation(ex[:, 512 * (u % 2):512 * (u % 2 + 1)],
                                           mn[:, 512 * (u % 2):512 * (u % 2 + 1)], AF.Exp))

            @blk.vector
            def _(v):
                for u in range(4):
                    v.wait_ge(s_mm.h, s_mm(u + 1))
                    if u >= 2:
                        v.wait_ge(s_ex2.h, s_ex2(u - 1))
                    s_mn.inc(v.tensor_scalar(mn[:, 512 * (u % 2):512 * (u % 2 + 1)],
                                             pq[u % 2][:], 0.0, None, op0=AL.min))
                    v.wait_ge(s_ex2.h, s_ex2(u + 1))
                    s_ev.inc(v.scalar_tensor_tensor(fq[:, 512 * u:512 * (u + 1)],
                                                    pq[u % 2][:], 0.0,
                                                    ex[:, 512 * (u % 2):512 * (u % 2 + 1)],
                                                    op0=AL.max, op1=AL.add))
                v.wait_ge(s_kl.h, kl_all)
                v.memset(vau[:].rearrange("p (a l e) -> p a l e", a=32, l=2)[:, :, :, 64:65], 1.0)
                v.drain()
                v.memset(ssb[:], 0.0)
                v.memset(l1["snA"][:, 0:130], 0.0)
                v.memset(l1["snB"][:, 0:130], 0.0)
                for j in range(NT):
                    aab, bab = ABLK[j // 4], BBLK[j // 4]
                    atA, atB = 4 * aab + (j % 4), 4 * bab + (j % 4)
                    v.tensor_scalar(fkf[:, P * j:P * (j + 1)], fkt[:, P * atA:P * (atA + 1)],
                                    flags[:, 0:1], None, op0=AL.mult)
                    v.tensor_scalar(vao[:, 130 * j:130 * (j + 1)], vau[:, 130 * atA:130 * (atA + 1)],
                                    flags[:, 0:1], None, op0=AL.mult)
                    v.drain()
                    v.scalar_tensor_tensor(vao[:, 130 * j:130 * (j + 1)],
                                           vau[:, 130 * atB:130 * (atB + 1)], flags[:, 1:2],
                                           vao[:, 130 * j:130 * (j + 1)],
                                           op0=AL.mult, op1=AL.add)
                    s_bl.inc(v.scalar_tensor_tensor(fkf[:, P * j:P * (j + 1)],
                                                    fkt[:, P * atB:P * (atB + 1)], flags[:, 1:2],
                                                    fkf[:, P * j:P * (j + 1)],
                                                    op0=AL.mult, op1=AL.add))
                for j in range(NT):
                    v.wait_ge(s_tr.h, s_tr(j + 1))
                    s_te.inc(v.tensor_copy(fkf[:, P * j:P * (j + 1)], ptt[j % 2][:]))

        with nc.Block(f"l1h{hp}b") as blk, ExitStack() as ps:
            aT = ps.enter_context(nc.sbuf_tensor(f"l1aT{hp}", [P, 1024], BF16))
            den = ps.enter_context(nc.sbuf_tensor(f"l1den{hp}", [P, 8], F32))
            psS = ps.enter_context(nc.psum_tensor(f"l1psS{hp}", [P, 512], F32))
            pa = ps.enter_context(nc.psum_tensor(f"l1pa{hp}", [P, 512], F32))
            pb = ps.enter_context(nc.psum_tensor(f"l1pb{hp}", [P, 512], F32))
            pcs = [ps.enter_context(nc.psum_tensor(f"l1pc{hp}{i}", [P, 512], F32)) for i in range(4)]
            fq = l1["fq"]; fkt = l1["fkt"]; vau = l1["vau"]; fkf = l1["fkf"]
            vao = l1["vao"]; ssb = l1["ssb"]; snA = l1["snA"]; snB = l1["snB"]
            s_su = sem(f"l1su{hp}"); s_sn = sem(f"l1sn{hp}")
            s_am = sem(f"l1am{hp}"); s_ae = sem(f"l1ae{hp}")
            s_cx = sem(f"l1cx{hp}"); s_ce = sem(f"l1ce{hp}")
            s_ini = sem(f"l1ini{hp}")

            @blk.vector
            def _(v):
                s_ini.inc(v.memset(psS[:], 0.0))

            @blk.tensor
            def _(te):
                te.wait_ge(s_ini.h, s_ini(1))
                for c in range(32):
                    if c % 4 == 0 and c > 0:
                        te.wait_ge(s_sn.h, s_sn(c // 4))
                    te.matmul(psS[0:64, 0:65], fkt[:, P * c:P * c + 64],
                              vau[:, 130 * c:130 * c + 65],
                              start=(c % 4 == 0), stop=(c % 4 == 3), tile_position=(0, 0),
                              skip_group_check=True)
                    i = te.matmul(psS[64:128, 0:65], fkt[:, P * c + 64:P * c + 128],
                                  vau[:, 130 * c + 65:130 * c + 130],
                                  start=(c % 4 == 0), stop=(c % 4 == 3), tile_position=(0, 64),
                                  skip_group_check=True)
                    s_su.inc(i)
                for g in range(4):
                    for c2 in range(4):
                        j2 = 4 * g + c2
                        te.wait_ge(s_ae.h, s_ae(8 * (4 * g + c2 - 1)) if (g, c2) != (0, 0) else 0)
                        for c1 in range(c2 + 1):
                            j1 = 4 * g + c1
                            te.matmul(pa[:, P * c1:P * (c1 + 1)], fkf[0:64, P * j1:P * (j1 + 1)],
                                      fq[0:64, P * j2:P * (j2 + 1)],
                                      start=True, stop=True, tile_position=(0, 0))
                            i = te.matmul(pb[:, P * c1:P * (c1 + 1)], fkf[64:128, P * j1:P * (j1 + 1)],
                                          fq[64:128, P * j2:P * (j2 + 1)],
                                          start=True, stop=True, tile_position=(64, 0))
                        s_am.inc(i)
                        if j2 >= 2:
                            te.wait_ge(s_ce.h, s_ce(j2 - 1))
                        pca, pcb = pcs[2 * (j2 % 2)], pcs[2 * (j2 % 2) + 1]
                        for hd, (pc, rows) in enumerate([(pca, (0, 64)), (pcb, (64, 128))]):
                            tp = (0, 0) if hd == 0 else (64, 0)
                            for si, sn in enumerate((snA, snB)):
                                for piece in range(2):
                                    st = (si == 0 and piece == 0)
                                    te.matmul(pc[rows[0]:rows[1], 0:65]
                                              if False else pc[:, 0:65],
                                              fq[rows[0]:rows[1], P * j2:P * (j2 + 1)],
                                              sn[rows[0]:rows[1],
                                                 130 * g + 65 * piece:130 * g + 65 * (piece + 1)],
                                              start=st, stop=False, tile_position=tp,
                                              skip_group_check=not st)
                        te.wait_ge(s_ae.h, s_ae(8 * (4 * g + c2) + 2 * (c2 + 1)))
                        for c1 in range(c2 + 1):
                            closer = (c1 == c2)
                            te.matmul(pca[:, 0:65], aT[:, P * c1:P * (c1 + 1)],
                                      vao[:, 130 * (4 * g + c1):130 * (4 * g + c1) + 65],
                                      start=False, stop=closer, skip_group_check=not closer)
                            i = te.matmul(pcb[:, 0:65], aT[:, 512 + P * c1:512 + P * (c1 + 1)],
                                          vao[:, 130 * (4 * g + c1) + 65:130 * (4 * g + c1) + 130],
                                          start=False, stop=closer, skip_group_check=not closer)
                        s_cx.inc(i)

            @blk.vector
            def _(v):
                for b in range(1, 8):
                    v.wait_ge(s_su.h, s_su(4 * b))
                    v.drain()
                    v.scalar_tensor_tensor(ssb[:, 0:65], psS[:, 0:65], 1.0, ssb[:, 0:65],
                                           op0=AL.mult, op1=AL.add)
                    v.drain()
                    if b in ABLK:
                        s0, dst, fcol = ABLK.index(b), snA, 0
                    else:
                        s0, dst, fcol = BBLK.index(b), snB, 1
                    v.tensor_scalar(dst[:, 130 * s0:130 * s0 + 65], ssb[:, 0:65],
                                    flags[:, fcol:fcol + 1], None, op0=AL.mult)
                    v.drain()
                    s_sn.inc(v.scalar_tensor_tensor(dst[:, 130 * s0 + 65:130 * s0 + 130],
                                                    ssb[:, 0:65], flags[:, fcol:fcol + 1],
                                                    dst[:, 130 * s0:130 * s0 + 65],
                                                    op0=AL.mult, op1=AL.subtract))
                for g in range(4):
                    for c2 in range(4):
                        j2 = 4 * g + c2
                        v.wait_ge(s_am.h, s_am(4 * g + c2 + 1 + sum(range(1)) * 0))
                        for c1 in range(c2 + 1):
                            if c1 == c2:
                                v.tensor_tensor(aT[:, P * c1:P * (c1 + 1)],
                                                pa[:, P * c1:P * (c1 + 1)], trimask[:], op=AL.mult)
                                i = v.tensor_tensor(aT[:, 512 + P * c1:512 + P * (c1 + 1)],
                                                    pb[:, P * c1:P * (c1 + 1)], trimask[:], op=AL.mult)
                            else:
                                v.tensor_copy(aT[:, P * c1:P * (c1 + 1)], pa[:, P * c1:P * (c1 + 1)])
                                i = v.tensor_copy(aT[:, 512 + P * c1:512 + P * (c1 + 1)],
                                                  pb[:, P * c1:P * (c1 + 1)])
                        s_ae.inc(i, 8 * 0 + 2 * (c2 + 1) + 8 * (4 * g + c2) - s_ae.v + s_ae.base)
                        pca, pcb = pcs[2 * (j2 % 2)], pcs[2 * (j2 % 2) + 1]
                        v.wait_ge(s_cx.h, s_cx(j2 + 1))
                        v.drain()
                        v.tensor_copy(den[:, 0:1], pca[:, 64:65])
                        v.tensor_copy(den[:, 1:2], pcb[:, 64:65])
                        v.drain()
                        v.tensor_scalar(den[:, 2:4], den[:, 0:2], 1e-6, None, op0=AL.max)
                        v.drain()
                        v.reciprocal(den[:, 4:6], den[:, 2:4])
                        v.drain()
                        v.tensor_scalar(l1["ctm"][:, D * j2 + P * hp:D * j2 + P * hp + 64],
                                        pca[:, 0:64], den[:, 4:5], None, op0=AL.mult)
                        i = v.tensor_scalar(l1["ctm"][:, D * j2 + P * hp + 64:D * j2 + P * hp + 128],
                                            pcb[:, 0:64], den[:, 5:6], None, op0=AL.mult)
                        s_ce.inc(i)

    def emit_l1_oproj():
        new_phase()
        with nc.Block("l1o") as blk, ExitStack() as ps:
            cfm = ps.enter_context(nc.sbuf_tensor("l1cfm", [P, NC * T], BF16))
            woT = ps.enter_context(nc.sbuf_tensor("l1woT", [P, NC * D], BF16))
            ptr = [ps.enter_context(nc.psum_tensor(f"l1ptr{i}", [P, P], BF16)) for i in range(4)]
            po = [ps.enter_context(nc.psum_tensor(f"l1po{i}", [P, 512], F32)) for i in range(2)]
            s_wl = sem("l1owl"); s_tr = sem("l1otr"); s_te = sem("l1ote")
            s_mm = sem("l1omm"); s_rs = sem("l1ors")
            ctm = l1["ctm"]

            @blk.sync
            def _(sy):
                wait_cast(sy, "wout")
                for c in range(NC):
                    s_wl.dinc(sy.dma_start_transpose(woT[:, D * c:D * (c + 1)],
                                                     wout_bf.ap()[:, P * c:P * (c + 1)]))

            @blk.tensor
            def _(te):
                for t in range(NT):
                    for c in range(NC):
                        k = NC * t + c
                        if k >= 4:
                            te.wait_ge(s_te.h, s_te(k - 3))
                        s_tr.inc(te.transpose(ptr[k % 4][:],
                                              ctm[:, D * t + P * c:D * t + P * (c + 1)],
                                              ident[:]))
                te.wait_ge(s_wl.h, s_wl(16 * NC))
                u = 0
                for tt in range(NT):
                    for oh in range(2):
                        if u >= 2:
                            te.wait_ge(s_rs.h, s_rs(u - 1))
                        for c in range(NC):
                            te.wait_ge(s_te.h, s_te(NC * tt + c + 1))
                            i = te.matmul(po[u % 2][:], cfm[:, T * c + P * tt:T * c + P * (tt + 1)],
                                          woT[:, D * c + 512 * oh:D * c + 512 * (oh + 1)],
                                          start=(c == 0), stop=(c == NC - 1))
                        s_mm.inc(i)
                        u += 1

            @blk.vector
            def _(v):
                for k in range(NT * NC):
                    v.wait_ge(s_tr.h, s_tr(k + 1))
                    t, c = k // NC, k % NC
                    s_te.inc(v.tensor_copy(cfm[:, T * c + P * t:T * c + P * (t + 1)], ptr[k % 4][:]))
                u = 0
                for tt in range(NT):
                    for oh in range(2):
                        v.wait_ge(s_mm.h, s_mm(u + 1))
                        s_rs.inc(v.scalar_tensor_tensor(
                            xres[:, D * tt + 512 * oh:D * tt + 512 * (oh + 1)],
                            po[u % 2][:], 1.0,
                            xres[:, D * tt + 512 * oh:D * tt + 512 * (oh + 1)],
                            op0=AL.mult, op1=AL.add))
                        u += 1

        l1["rs_done"] = (s_rs.h, s_rs.v)

    def emit_l1_attn():
        with ExitStack() as les:
            emit_l1_kv(les)
            l1["fq"] = les.enter_context(nc.sbuf_tensor("l1fq", [P, T], BF16))
            l1["fkt"] = les.enter_context(nc.sbuf_tensor("l1fkt", [P, 32 * P], BF16))
            l1["vau"] = les.enter_context(nc.sbuf_tensor("l1vau", [P, 32 * 130], BF16))
            l1["fkf"] = les.enter_context(nc.sbuf_tensor("l1fkf", [P, NT * P], BF16))
            l1["vao"] = les.enter_context(nc.sbuf_tensor("l1vao", [P, NT * 130], BF16))
            l1["ssb"] = les.enter_context(nc.sbuf_tensor("l1ssb", [P, 65], F32))
            l1["snA"] = les.enter_context(nc.sbuf_tensor("l1snA", [P, 4 * 130], BF16))
            l1["snB"] = les.enter_context(nc.sbuf_tensor("l1snB", [P, 4 * 130], BF16))
            for hs in range(2):
                for hpl in range(4):
                    emit_l1_hp(hs, hpl)
            emit_l1_oproj()

    fin = {}

    def emit_l1_ffn():
        new_phase()
        NM = 32
        NJ = 32
        NS = 4
        with nc.Block("l1f") as blk, ExitStack() as ps:
            bufA = ps.enter_context(nc.sbuf_tensor("f1bufA", [P, NJ * 512], BF16))
            bufB = ps.enter_context(nc.sbuf_tensor("f1bufB", [P, NJ * 512], BF16))
            wpA = ps.enter_context(nc.sbuf_tensor("f1wpA", [P, 3 * 1024], BF16))
            wpH = ps.enter_context(nc.sbuf_tensor("f1wpH", [P, 3 * 2048], BF16))
            w3r = ps.enter_context(nc.sbuf_tensor("f1w3r", [P, 4 * 512], BF16))
            scrF = ps.enter_context(nc.sbuf_tensor("f1scr", [P, 2 * 512], F32))
            scl = ps.enter_context(nc.sbuf_tensor("f1scl", [P, 4 * 32], F32))
            s3row = ps.enter_context(nc.sbuf_tensor("f1s3r", [1, D], F32))
            ones1 = ps.enter_context(nc.sbuf_tensor("f1one", [1, P], F32))
            s3rep = ps.enter_context(nc.sbuf_tensor("f1s3b", [P, D], F32))
            p4 = [ps.enter_context(nc.psum_tensor(f"f1p4{i}", [P, 512], F32)) for i in range(4)]
            pg = [ps.enter_context(nc.psum_tensor(f"f1pg{i}", [P, 512], F32)) for i in range(2)]
            s_pa = sem("f1pa"); s_phs = sem("f1ps"); s_pha = sem("f1pv")
            s_dw = sem("f1dw"); s_scl = sem("f1sc")
            s_am1 = sem("f1a1"); s_am2 = sem("f1a2"); s_bm = sem("f1bm"); s_cm = sem("f1cm")
            s_dmm = sem("f1dm"); s_s3m = sem("f1s3")
            s_asl = sem("f1as"); s_bg = sem("f1bg"); s_cg = sem("f1cg")
            s_afu = sem("f1af"); s_dres = sem("f1dr"); s_s3c = sem("f1s3c"); s_o = sem("f1o")

            def hbuf(s):
                return bufA if s % 2 == 0 else bufB

            def gbuf(s):
                return bufB if s % 2 == 0 else bufA

            cw1, vw1 = PANEL_AT["w1"]; cw2, vw2 = PANEL_AT["w2"]
            ch1, vh1 = PANEL_AT["h1"]; ch2, vh2 = PANEL_AT["h2"]
            cw3, vw3 = PANEL_AT["w3"]
            HP = ((ch1, vh1, h1_bf), (ch2, vh2, h2_bf))

            @blk.sync
            def _(sy):
                for i, src in enumerate((s1_in, s2_in, hs1_in, hs2_in)):
                    s_scl.dinc(sy.dma_start(out=scl[:, 32 * i:32 * (i + 1)],
                                            in_=src.rearrange("(m p) -> p m", p=P),
                                            allow_slow_non_contiguous=True))
                s_scl.dinc(sy.dma_start(out=s3row[0:1, :],
                                        in_=s3_in.rearrange("(o d) -> o d", o=1)))
                for s in range(NS):
                    for m in range(NM):
                        g = NM * s + m
                        for w, (cs, vals, wbf) in enumerate(((cw1, vw1, w1_bf),
                                                             (cw2, vw2, w2_bf))):
                            q = 2 * g + w
                            if q >= 3:
                                qp = q - 3
                                mark = s_am1 if qp % 2 == 0 else s_am2
                                sy.wait_ge(mark.h, mark(qp // 2 + 1))
                            sy.wait_ge(cs.h, vals[m])
                            s_pa.dinc(sy.dma_start(
                                out=wpA[:, 1024 * (q % 3):1024 * (q % 3) + 1024]
                                    .rearrange("p (k o) -> p k o", k=NC),
                                in_=wbf.ap()[m].rearrange("(k p) o -> p k o", p=P)))
                    for st in range(2):
                        cs, vals, wbf = HP[st]
                        own = s_bm if st == 0 else s_cm
                        for m in range(NM):
                            q = 2 * m
                            if q >= 3:
                                sy.wait_ge(own.h, own(NM * s + (q - 3) // 2 + 1))
                            elif st == 0:
                                if s > 0:
                                    sy.wait_ge(s_cm.h, s_cm(NM * s))
                            else:
                                sy.wait_ge(s_bm.h, s_bm(NM * (s + 1)))
                            sy.wait_ge(cs.h, vals[m])
                            s_phs.dinc(sy.dma_start(
                                out=wpH[:, 2048 * (q % 3):2048 * (q % 3) + 2048]
                                    .rearrange("p (k o) -> p k o", k=16),
                                in_=wbf.ap()[m, 0:2048, :].rearrange("(k p) o -> p k o", p=P)))

            @blk.scalar
            def _(a):
                def hpanel(st, s, m):
                    cs, vals, wbf = HP[st]
                    own = s_bm if st == 0 else s_cm
                    q = 2 * m + 1
                    if q >= 3:
                        a.wait_ge(own.h, own(NM * s + (q - 3) // 2 + 1))
                    elif st == 0:
                        if s > 0:
                            a.wait_ge(s_cm.h, s_cm(NM * s))
                    else:
                        a.wait_ge(s_bm.h, s_bm(NM * (s + 1)))
                    a.wait_ge(cs.h, vals[m])
                    s_pha.dinc(a.dma_start(
                        out=wpH[:, 2048 * (q % 3):2048 * (q % 3) + 2048]
                            .rearrange("p (k o) -> p k o", k=16),
                        in_=wbf.ap()[m, 2048:4096, :].rearrange("(k p) o -> p k o", p=P)))

                for s in range(NS):
                    hb, gb = hbuf(s), gbuf(s)
                    for m in range(NM):
                        g = NM * s + m
                        a.wait_ge(s_am1.h, s_am1(g + 1))
                        if g == 0:
                            a.wait_ge(s_scl.h, s_scl(16 * 5))
                        if g >= 2:
                            a.wait_ge(s_afu.h, s_afu(g - 1))
                        s_asl.inc(a.activation(scrF[:, 512 * (m % 2):512 * (m % 2) + 512],
                                               p4[m % 2][:], AF.Silu, scale=scl[:, m:m + 1]))
                    for m in range(NM + 2):
                        if m >= 2:
                            mm = m - 2
                            a.wait_ge(s_bm.h, s_bm(NM * s + mm + 1))
                            s_bg.inc(a.activation(gb[:, 512 * mm:512 * (mm + 1)], pg[mm % 2][:],
                                                  AF.Gelu, scale=scl[:, 64 + mm:64 + mm + 1]))
                        if m < NM:
                            hpanel(0, s, m)
                    for j in range(4):
                        dj = 64 * s + j
                        if dj >= 4:
                            a.wait_ge(s_dmm.h, s_dmm(dj - 3))
                        if s == 0 and j == 0:
                            a.wait_ge(cw3.h, vw3[0])
                        s_dw.dinc(a.dma_start(out=w3r[:, 512 * (j % 4):512 * (j % 4) + 512],
                                              in_=w3_bf.ap()[0, P * j:P * (j + 1), :]))
                    for m in range(NM + 2):
                        if m >= 2:
                            mm = m - 2
                            a.wait_ge(s_cm.h, s_cm(NM * s + mm + 1))
                            s_cg.inc(a.activation(hb[:, 512 * mm:512 * (mm + 1)], pg[mm % 2][:],
                                                  AF.Gelu, scale=scl[:, 96 + mm:96 + mm + 1]))
                        if m < NM:
                            hpanel(1, s, m)
                    for h in range(2):
                        for j in range(NJ):
                            if h == 0 and j < 4:
                                continue
                            dj = 64 * s + 32 * h + j
                            a.wait_ge(s_dmm.h, s_dmm(dj - 3))
                            if s == 0 and h == 1 and j == 0:
                                a.wait_ge(cw3.h, vw3[1])
                            s_dw.dinc(a.dma_start(out=w3r[:, 512 * (j % 4):512 * (j % 4) + 512],
                                                  in_=w3_bf.ap()[h, P * j:P * (j + 1), :]))

            @blk.vector
            def _(v):
                s_o.inc(v.memset(ones1[0:1, :], 1.0))
                v.wait_ge(s_s3m.h, s_s3m(2))
                v.tensor_copy(s3rep[:, 0:512], pg[0][:])
                s_s3c.inc(v.drain())
                v.tensor_copy(s3rep[:, 512:1024], pg[1][:])
                s_s3c.inc(v.drain())
                for s in range(NS):
                    hb, gb = hbuf(s), gbuf(s)
                    for m in range(NM):
                        g = NM * s + m
                        v.wait_ge(s_am2.h, s_am2(g + 1))
                        v.wait_ge(s_asl.h, s_asl(g + 1))
                        s_afu.inc(v.scalar_tensor_tensor(
                            hb[:, 512 * m:512 * (m + 1)], p4[2 + m % 2][:],
                            scl[:, 32 + m:32 + m + 1],
                            scrF[:, 512 * (m % 2):512 * (m % 2) + 512],
                            op0=AL.mult, op1=AL.mult))
                    for h in range(2):
                        v.wait_ge(s_dmm.h, s_dmm(64 * s + 32 * (h + 1)))
                        for t in range(4):
                            xs = xres[:, D * (4 * s + t) + 512 * h:D * (4 * s + t) + 512 * h + 512]
                            v.tensor_tensor(scrF[:, 512 * (t % 2):512 * (t % 2) + 512],
                                            p4[t][:], s3rep[:, 512 * h:512 * (h + 1)],
                                            op=AL.mult)
                            v.drain()
                            s_dres.inc(v.tensor_tensor(
                                xs, scrF[:, 512 * (t % 2):512 * (t % 2) + 512], xs, op=AL.add))

            @blk.tensor
            def _(te):
                te.wait_ge(s_o.h, s_o(1))
                te.wait_ge(s_scl.h, s_scl(16 * 5))
                for hh in range(2):
                    s_s3m.inc(te.matmul(pg[hh][:], ones1[0:1, :],
                                        s3row[0:1, 512 * hh:512 * (hh + 1)],
                                        start=True, stop=True))
                for s in range(NS):
                    hb, gb = hbuf(s), gbuf(s)
                    for m in range(NM):
                        g = NM * s + m
                        if g == 0:
                            te.wait_ge(s_s3c.h, s_s3c(2))
                        if m == 0 and s > 0:
                            te.wait_ge(s_dres.h, s_dres(8 * s))
                        if g >= 2:
                            te.wait_ge(s_asl.h, s_asl(g - 1))
                            te.wait_ge(s_afu.h, s_afu(g - 1))
                        q1, q2 = 2 * g, 2 * g + 1
                        te.wait_ge(s_pa.h, s_pa(16 * (q1 + 1)))
                        for k in range(NC):
                            i = te.matmul(p4[m % 2][:],
                                          wpA[:, 1024 * (q1 % 3) + P * k:1024 * (q1 % 3) + P * (k + 1)],
                                          xln[:, T * k + 512 * s:T * k + 512 * s + 512],
                                          start=(k == 0), stop=(k == NC - 1))
                        s_am1.inc(i)
                        te.wait_ge(s_pa.h, s_pa(16 * (q2 + 1)))
                        for k in range(NC):
                            i = te.matmul(p4[2 + m % 2][:],
                                          wpA[:, 1024 * (q2 % 3) + P * k:1024 * (q2 % 3) + P * (k + 1)],
                                          xln[:, T * k + 512 * s:T * k + 512 * s + 512],
                                          start=(k == 0), stop=(k == NC - 1))
                        s_am2.inc(i)
                    for m in range(NM):
                        if m == 0:
                            te.wait_ge(s_afu.h, s_afu(NM * (s + 1)))
                            if s > 0:
                                te.wait_ge(s_cg.h, s_cg(NM * s))
                        if m >= 2:
                            te.wait_ge(s_bg.h, s_bg(NM * s + m - 1))
                        for half in range(2):
                            qq = 2 * m + half
                            wsem = s_phs if half == 0 else s_pha
                            te.wait_ge(wsem.h, wsem(16 * (2 * s * NM + m + 1)))
                            for jj in range(16):
                                j = 16 * half + jj
                                i = te.matmul(pg[m % 2][:],
                                              wpH[:, 2048 * (qq % 3) + P * jj:2048 * (qq % 3) + P * (jj + 1)],
                                              hb[:, 512 * j:512 * (j + 1)],
                                              start=(j == 0), stop=(j == NJ - 1))
                        s_bm.inc(i)
                    for m in range(NM):
                        if m == 0:
                            te.wait_ge(s_bg.h, s_bg(NM * (s + 1)))
                        if m >= 2:
                            te.wait_ge(s_cg.h, s_cg(NM * s + m - 1))
                        for half in range(2):
                            qq = 2 * m + half
                            wsem = s_phs if half == 0 else s_pha
                            te.wait_ge(wsem.h, wsem(16 * ((2 * s + 1) * NM + m + 1)))
                            for jj in range(16):
                                j = 16 * half + jj
                                i = te.matmul(pg[m % 2][:],
                                              wpH[:, 2048 * (qq % 3) + P * jj:2048 * (qq % 3) + P * (jj + 1)],
                                              gb[:, 512 * j:512 * (j + 1)],
                                              start=(j == 0), stop=(j == NJ - 1))
                        s_cm.inc(i)
                    for h in range(2):
                        for j in range(NJ):
                            dj = 64 * s + 32 * h + j
                            if h == 0 and j == 0:
                                te.wait_ge(s_cg.h, s_cg(NM * (s + 1)))
                                te.wait_ge(s_asl.h, s_asl(NM * (s + 1)))
                            if h == 1 and j == 0:
                                te.wait_ge(s_dres.h, s_dres(8 * s + 4))
                            te.wait_ge(s_dw.h, s_dw(16 * (dj + 1)))
                            for t in range(4):
                                i = te.matmul(p4[t][:],
                                              hb[:, 512 * j + P * t:512 * j + P * (t + 1)],
                                              w3r[:, 512 * (j % 4):512 * (j % 4) + 512],
                                              start=(j == 0), stop=(j == NJ - 1))
                            s_dmm.inc(i)

        fin["done"] = (s_dres.h, s_dres.v)

    emit_ln("1")
    l0sub = [p[2:] for p in phases if p.startswith("l0") and p != "l0ffn"]
    if l0sub:
        emit_l0_attn([s for s in l0sub if s != "attn"] if l0sub != ["attn"] else
                     ["h0","h1","h2","h3","h4","h5","h6","h7","o0","o1"])
    if "l0ffn" in phases:
        with ExitStack() as les:
            emit_l0_wtt(les)
            emit_ln("2", fence=l0.get("rs_done"))
            emit_l0_ffn()
    if "l1attn" in phases:
        emit_ln("3", fence=l0f.get("rs_done"))
        emit_l1_attn()
    if "l1ffn" in phases:
        emit_ln("4", fence=l1.get("rs_done"))
        emit_l1_ffn()

    fence_y = (fin.get("done") or l1.get("rs_done") or l0f.get("rs_done")
               or l0.get("rs_done"))
    with nc.Block("tail") as blk:
        @blk.sync
        def _(sy):
            if fence_y is not None:
                sy.wait_ge(fence_y[0], fence_y[1])
            if "xln" in tapout:
                s_st.dinc(sy.dma_start(out=tapout["xln"], in_=xln[:]))
            if "lnstat" in tapout:
                s_st.dinc(sy.dma_start(out=tapout["lnstat"], in_=ln_stat[:]))
            if "ctm" in tapout:
                s_st.dinc(sy.dma_start(out=tapout["ctm"], in_=l0["ctm"][:]))
            s_st.dinc(sy.dma_start(out=y_out.rearrange("(t p) d -> p t d", p=P), in_=xres[:]))
            sy.wait_ge(s_st.h, s_st.v)

    es.close()
    return nc, tapout


def make_constants():
    ident = np.eye(P, dtype=ml_dtypes.bfloat16)
    tri = (np.arange(P)[:, None] <= np.arange(P)[None, :]).astype(ml_dtypes.bfloat16)
    return ident, tri


def make_masks_l0(blocks):
    m = np.zeros((NSLOT, 2, 4, P, 256), dtype=np.float32)
    for s in range(NSLOT):
        blk = blocks[s // 2]
        a_s = 512 * blk + 256 * (s % 2)
        pext = PADEXT[s]
        for j, kg in enumerate([pext - 2, pext - 1]):
            for kc in range(4):
                kabs = 512 * kg + 128 * kc + np.arange(P)[:, None]
                qabs = a_s + np.arange(256)[None, :]
                m[s, j, kc] = (kabs <= qabs)
    return m.astype(ml_dtypes.bfloat16)


def _pan(w, blk_out=128):
    wt = np.ascontiguousarray(np.asarray(w)).T
    nin, nout = wt.shape
    return np.ascontiguousarray(wt.reshape(nin, nout // blk_out, blk_out).transpose(1, 0, 2))


def shard_inputs(ins):
    ident, tri = make_constants()
    in_maps = []
    w1p = _pan(ins["L1_ffn_w1_w"]); w2p = _pan(ins["L1_ffn_w2_w"])
    h1p = _pan(ins["L1_ffn_h1_w"]); h2p = _pan(ins["L1_ffn_h2_w"])
    w3p = _pan(ins["L1_ffn_w3_w"], 512)
    x = np.asarray(ins["L0_attn_x"])
    for core in range(8):
        seq, half = core // 2, core % 2
        blocks = ABLK if half == 0 else BBLK
        xs = np.concatenate([x[seq, 512 * b:512 * (b + 1)] for b in blocks], axis=0)
        flags = np.zeros((P, 2), np.float32)
        flags[:, half] = 1.0
        m = {
            "x": np.ascontiguousarray(xs),
            "wq": ins["L0_attn_Wq"], "wk": ins["L0_attn_Wk"], "wv": ins["L0_attn_Wv"],
            "wo": ins["L0_attn_Wo"], "bo": ins["L0_attn_bo"],
            "g1r": ins["L0_ffn_G1_R"], "g2r": ins["L0_ffn_G2_R"],
            "g1i": ins["L0_ffn_G1_I"], "g2i": ins["L0_ffn_G2_I"],
            "br": ins["L0_ffn_bias_real"], "bi": ins["L0_ffn_bias_imag"],
            "wqkv": ins["L1_attn_w_qkv"], "wout": ins["L1_attn_w_out"],
            "w1": w1p, "s1": ins["L1_ffn_w1_s"],
            "w2": w2p, "s2": ins["L1_ffn_w2_s"],
            "h1": h1p, "hs1": ins["L1_ffn_h1_s"],
            "h2": h2p, "hs2": ins["L1_ffn_h2_s"],
            "w3": w3p, "s3": ins["L1_ffn_w3_s"],
            "maskl0": make_masks_l0(blocks),
            "identc": ident, "trimask": tri,
            "flags": flags,
        }
        in_maps.append({k: np.asarray(v) for k, v in m.items()})
    return in_maps


def assemble_output(results):
    y = np.zeros((4, L, D), np.float32)
    for core in range(8):
        seq, half = core // 2, core % 2
        blocks = ABLK if half == 0 else BBLK
        yc = results[core]["y"]
        for j, b in enumerate(blocks):
            y[seq, 512 * b:512 * (b + 1)] = yc[512 * j:512 * (j + 1)]
    return y


def kernel(**inputs):
    nc, _ = build()
    in_maps = shard_inputs(inputs)
    res = run_bass_kernel_spmd(nc, in_maps, core_ids=list(range(8)))
    return assemble_output(res.results)
